# version 02 (final)
# baseline (speedup 1.0000x reference)
import contextlib
import numpy as np
import concourse.bass as bass
import concourse.mybir as mybir
from concourse.bass_utils import run_bass_kernel_spmd

F32 = mybir.dt.float32
BF16 = mybir.dt.bfloat16
AF = mybir.ActivationFunctionType
ALU = mybir.AluOpType
AX = mybir.AxisListType

NCORES = 8
S = 4096
D = 2048
HKV = 4
H = 16
DH = 128
F = 768
CH = 128
TILE = 512
NCHT = TILE // CH
NT = S // TILE
EPS = 1e-8
LNEPS = 1e-5
ZC = 1.0 / D

ENGS = ("sync", "gpsimd", "tensor", "vector", "scalar")


class Sched:

    def __init__(self):
        self.prog = {e: [] for e in ENGS}
        self.cnt = {}
        self.lastwait = {}
        self.pending_barrier = {}

    def val(self, sem):
        return self.cnt.get(sem, 0)

    def barrier(self, waits):
        for e in ENGS:
            self.pending_barrier[e] = list(waits)

    def op(self, eng, fn, waits=(), inc=None, incn=1, tight=False):
        ws = list(self.pending_barrier.pop(eng, [])) + list(waits)
        incs = []
        if tight:
            ws.append(("szz", self.cnt.get("szz", 0)))
            incs.append(("szz", 1))
        if inc is not None:
            incs.append((inc, incn))
        ws2 = []
        for sem, v in ws:
            if v > 0 and self.lastwait.get((eng, sem), 0) < v:
                ws2.append((sem, v))
                self.lastwait[(eng, sem)] = v
        self.prog[eng].append((ws2, fn, incs))
        ret = None
        for sem, n in incs:
            self.cnt[sem] = self.cnt.get(sem, 0) + n
            if sem == inc:
                ret = self.cnt[sem]
        return ret

    def replay(self, eng_name, eng, sems):
        for ws, fn, incs in self.prog[eng_name]:
            for sem, v in ws:
                eng.wait_ge(sems[sem], v)
            inst = fn(eng)
            if incs:
                assert inst is not None, f"op with incs returned no inst ({eng_name})"
                inst.then_inc(sems[incs[0][0]], incs[0][1])
                for sem, n in incs[1:]:
                    eng.nop().then_inc(sems[sem], n)



def build_nc():
    nc = bass.Bass("TRN2", target_bir_lowering=False)
    es = contextlib.ExitStack()

    xin = nc.dram_tensor("x", [S, D], F32, kind="ExternalInput").ap()
    wkT = nc.dram_tensor("wkT", [D, 512], F32, kind="ExternalInput").ap()
    wvT = nc.dram_tensor("wvT", [D, 512], F32, kind="ExternalInput").ap()
    wmbT = nc.dram_tensor("wmbT", [D, D], F32, kind="ExternalInput").ap()
    wqT = nc.dram_tensor("wqT", [D, D], F32, kind="ExternalInput").ap()
    wgT = nc.dram_tensor("wgT", [D, D], F32, kind="ExternalInput").ap()
    wfT = nc.dram_tensor("wfT", [D, D], F32, kind="ExternalInput").ap()
    varT = nc.dram_tensor("varT", [D, D], F32, kind="ExternalInput").ap()
    biases = nc.dram_tensor("biases", [1, 7168], F32, kind="ExternalInput").ap()
    memin = nc.dram_tensor("memory", [H, F, DH], F32, kind="ExternalInput").ap()
    normT = nc.dram_tensor("normT", [F, H], F32, kind="ExternalInput").ap()
    clocal = nc.dram_tensor("clocal", [128, 16], F32, kind="ExternalInput").ap()
    wmodK = nc.dram_tensor("wmodK", [256, 12288], F32, kind="ExternalInput").ap()
    ident = nc.dram_tensor("ident", [128, 128], F32, kind="ExternalInput").ap()
    out = nc.dram_tensor("out", [S, D], F32, kind="ExternalOutput").ap()

    wmb_s = nc.dram_tensor("wmb_s", [D, D], BF16)
    wq_s = nc.dram_tensor("wq_s", [D, D], BF16)
    wg_s = nc.dram_tensor("wg_s", [D, D], BF16)
    wf_s = nc.dram_tensor("wf_s", [D, D], BF16)
    rs1_in = nc.dram_tensor("rs1_in", [8, 12288], F32)
    rs1_out = nc.dram_tensor("rs1_out", [1, 12288], F32)
    ar2_in = nc.dram_tensor("ar2_in", [129, 12288], F32)
    dpart = nc.dram_tensor("dpart", [NT, 129, 12288], F32)
    ar2_out = nc.dram_tensor("ar2_out", [129, 12288], F32)

    sch = Sched()
    SEMNAMES = ["sx", "sgd", "st", "sv", "ss", "scc", "sout", "szz", "sxc", "smb", "sda", "swc"]

    def sb(name, shape, dtype, stack=None):
        return (stack or es).enter_context(nc.sbuf_tensor(name, shape, dtype))

    def psum(name, shape, dtype, stack=None):
        return (stack or es).enter_context(nc.psum_tensor(name, shape, dtype))

    def cur(sem):
        return (sem, sch.val(sem))

    def babylonian(v_ap, out_s, vh_scr, rs_scr, s2_scr, r2_scr, n_iter, waits=()):
        if n_iter % 2 == 1:
            n_iter += 1
        sch.op("vector", lambda v, a=v_ap: v.tensor_scalar(s2_scr, a, 1e-24, None, ALU.max),
               waits=waits, tight=True)
        sch.op("vector", lambda v: v.tensor_scalar(vh_scr, s2_scr, 0.5, None, ALU.mult), tight=True)
        sch.op("vector", lambda v: v.tensor_scalar(out_s, vh_scr, 0.5, None, ALU.add), tight=True)
        cur_s, nxt_s = out_s, s2_scr
        for _ in range(n_iter):
            sch.op("vector", lambda v, c=cur_s: v.reciprocal(rs_scr, c), tight=True)
            sch.op("vector", lambda v: v.tensor_tensor(r2_scr, vh_scr, rs_scr, ALU.mult), tight=True)
            sch.op("vector", lambda v, c=cur_s, nx=nxt_s: v.scalar_tensor_tensor(
                nx, c, 0.5, r2_scr, ALU.mult, ALU.add), tight=True)
            cur_s, nxt_s = nxt_s, cur_s
        assert cur_s is out_s

    stats = sb("stats", [128, 64], F32)
    A16 = sb("A16", [128, D], BF16)
    sa16 = sb("sa16", [128, D], BF16)
    ones1 = sb("ones1", [1, 128], BF16)
    onesf = sb("onesf", [1, 128], F32)
    identb = sb("identb", [128, 128], BF16)
    identf = sb("identf", [128, 128], F32)
    tiny = sb("tiny", [128, 128], F32)
    tiny2 = sb("tiny2", [128, 128], F32)
    mark = sb("mark", [128, 8], F32)

    p1 = contextlib.ExitStack()
    wkv = sb("wkv", [128, 16 * 1024], BF16, p1)
    wmb_sl = [sb(f"wmbsl{i}", [128, 16 * 256], BF16, p1) for i in range(2)]
    mem_bl = [sb(f"membl{i}", [128, 6 * 512], BF16, p1) for i in range(2)]
    normb = sb("normb", [128, 6 * 16], BF16, p1)
    b1 = sb("b1", [1, 3072], BF16, p1)
    xbf = [sb(f"xbf{i}", [128, D], BF16, p1) for i in range(2)]
    h_bf = sb("h_bf", [128, D], BF16, p1)
    h_tmp = sb("h_tmp", [128, D], BF16, p1)
    hfm = sb("hfm", [128, 2 * D], BF16, p1)
    xext = sb("xext", [128, HKV * 264], BF16, p1)
    dpk = sb("dpk", [128, NCHT * HKV * F], BF16, p1)
    vt = sb("vt", [128, NCHT * 512], BF16, p1)
    mbt = sb("mbt", [128, NCHT * D], BF16, p1)
    wmvt = sb("wmvt", [128, NCHT * D], BF16, p1)
    dpkfm = sb("dpkfm", [128, F], BF16, p1)
    prevb = sb("prevb", [128, 512], BF16, p1)
    mvb = sb("mvb", [128, 128], BF16, p1)
    dstg = [sb(f"dstg{i}", [128, F], F32, p1) for i in range(2)]
    n2k = sb("n2k", [128, 16], F32, p1)
    nk = sb("nk", [128, 16], F32, p1)
    rk = sb("rk", [128, 16], F32, p1)
    epsn = sb("epsn", [128, 16], F32, p1)
    c_t = sb("c_t", [128, 64], BF16, p1)
    den16 = sb("den16", [128, 16], F32, p1)
    rden = sb("rden", [128, 16], F32, p1)
    trash7 = sb("trash7", [128, F], BF16, p1)

    ps_k = psum("ps_k", [128, 512], F32, p1)
    ps_v = psum("ps_v", [128, 512], F32, p1)
    ps_m1 = psum("ps_m1", [128, 512], F32, p1)
    ps_m2 = psum("ps_m2", [128, 512], F32, p1)
    ps_num = psum("ps_num", [128, 512], F32, p1)
    ps_den = psum("ps_den", [128, 512], F32, p1)
    ps_tr = psum("ps_tr", [128, 1024], BF16, p1)
    ps_dl = psum("ps_dl", [128, 512], F32, p1)

    st0 = contextlib.ExitStack()
    mpstg = [sb(f"mpstg{i}", [8, 512], F32, st0) for i in range(2)]
    cA = sb("cA", [128, 16], F32, st0)
    sigc = sb("sigc", [128, 16], F32, st0)
    siluc = sb("siluc", [128, 16], F32, st0)
    wsl = [sb(f"wsl{i}", [128, 512], F32, st0) for i in range(2)]
    vrow = sb("vrow", [1, D], F32, st0)
    vrow16 = xbf[0]

    sch.op("vector", lambda v: v.memset(ones1[:, :], 1.0))
    sch.op("vector", lambda v: v.memset(onesf[:, :], 1.0))
    sch.op("vector", lambda v: v.memset(tiny[:, :], 0.0))
    sch.op("vector", lambda v: v.memset(tiny2[:, :], 0.0))
    sch.op("vector", lambda v: v.memset(mark[:, :], 0.0), inc="sv")
    v_dn0 = sch.val("sv")
    sch.op("gpsimd", lambda g: g.dma_start(out=identb[:, :], in_=ident), inc="sgd", incn=16)
    g_ident = sch.val("sgd")
    sch.op("sync", lambda s: s.dma_start(out=identf[:, :], in_=ident), inc="sx", incn=16)
    sch.op("gpsimd", lambda g: g.dma_start(out=b1[0:1, 0:3072], in_=biases[0:1, 0:3072]), inc="sgd", incn=16)
    sch.op("sync", lambda s: s.dma_start(out=cA[:, :], in_=clocal), inc="sx", incn=16)
    x_cA = sch.val("sx")
    for fc in range(16):
        sch.op("gpsimd", lambda g, fc=fc: g.dma_start(
            out=wkv[:, fc * 1024: fc * 1024 + 512], in_=wkT[fc * 128:(fc + 1) * 128, :]), inc="sgd", incn=16)
        sch.op("gpsimd", lambda g, fc=fc: g.dma_start(
            out=wkv[:, fc * 1024 + 512: fc * 1024 + 1024], in_=wvT[fc * 128:(fc + 1) * 128, :]), inc="sgd", incn=16)
    for fc in range(6):
        sch.op("gpsimd", lambda g, fc=fc: g.dma_start(
            out=normb[:, fc * 16:(fc + 1) * 16], in_=normT[fc * 128:(fc + 1) * 128, :]), inc="sgd", incn=16)
    g_wkv = sch.val("sgd")

    sch.op("scalar", lambda s: s.activation(sigc[:, :], cA[:, :], AF.Sigmoid),
           waits=[("sx", x_cA)], inc="ss")
    sch.op("vector", lambda v: v.tensor_tensor(siluc[:, :], cA[:, :], sigc[:, :], ALU.mult),
           waits=[cur("ss")], inc="sv")
    v_silu = sch.val("sv")

    evac_v = 0
    load_x = [0, 0]
    mp_users = [0, 0]
    for grp in range(24):
        ldv = [0, 0]
        for j in range(2):
            i = j
            ldv[i] = sch.op("sync", lambda s, grp=grp, j=j, i=i: s.dma_start(
                out=wsl[i][:, :], in_=wmodK[j * 128:(j + 1) * 128, grp * 512:(grp + 1) * 512]),
                waits=[("st", load_x[i])], inc="sx", incn=16)
        tmm = None
        for j in range(2):
            tmm = sch.op("tensor", lambda t, j=j: t.matmul(
                ps_num[0:8, :], siluc[:, j * 8:(j + 1) * 8], wsl[j][:, :],
                start=(j == 0), stop=(j == 1)),
                waits=[("sx", ldv[j]), ("sv", v_silu), ("sv", evac_v)],
                inc=("st" if j == 1 else None))
        load_x = [tmm, tmm]
        mi = grp % 2
        sch.op("vector", lambda v, grp=grp, mi=mi: v.tensor_copy(mpstg[mi][:, :], ps_num[0:8, :]),
               waits=[("st", tmm), ("sgd", mp_users[mi])], inc="sv")
        evac_v = sch.val("sv")
        sch.op("gpsimd", lambda g, grp=grp, mi=mi: g.dma_start(
            out=rs1_in.ap()[:, grp * 512:(grp + 1) * 512], in_=mpstg[mi][:, :]),
            waits=[("sv", evac_v)], inc="sgd", incn=16)
        mp_users[mi] = sch.val("sgd")

    sch.op("gpsimd", lambda g: g.collective_compute(
        "ReduceScatter", ALU.add, replica_groups=[list(range(NCORES))],
        ins=[rs1_in.ap().opt()], outs=[rs1_out.ap().opt()]),
        waits=[cur("sgd")], inc="scc")

    def load_seg(seg):
        sch.op("gpsimd", lambda g, seg=seg: g.dma_start(
            out=vrow[0:1, :], in_=rs1_out.ap()[0:1, seg * D:(seg + 1) * D]),
            waits=[cur("scc"), cur("sv")], inc="sgd", incn=16)
        return sch.val("sgd")

    def bcast_p1(seg, addone, dst):
        g_seg = load_seg(seg)
        sch.op("vector", lambda v, addone=addone: v.tensor_scalar(
            vrow16[0:1, 0:D], vrow[0:1, :], 1.0 if addone else 0.0, None, ALU.add),
            waits=[("sgd", g_seg)], inc="sv")
        for gq in range(4):
            tm = sch.op("tensor", lambda t, gq=gq: t.matmul(
                ps_k[:, :], ones1[0:1, :], vrow16[0:1, gq * 512:(gq + 1) * 512], start=True, stop=True),
                waits=[("sv", sch.val("sv"))], inc="st")
            sch.op("vector", lambda v, gq=gq, dst=dst: v.tensor_copy(dst[:, gq * 512:(gq + 1) * 512], ps_k[:, :]),
                   waits=[("st", tm)], inc="sv")
        return sch.val("sv")

    bcast_p1(1, True, A16)
    v_bcasts = bcast_p1(0, False, sa16)

    for rb in range(16):
        sch.op("gpsimd", lambda g, rb=rb: g.dma_start(
            out=wmb_s.ap()[rb * 128:(rb + 1) * 128, :], in_=wmbT[rb * 128:(rb + 1) * 128, :]),
            inc="swc", incn=16)
    g_wmb_done = sch.val("swc")

    def emit_mem_block(gq, buf_i, wait0):
        for fc in range(6):
            for jj in range(4):
                hh = 4 * gq + jj
                sch.op("gpsimd", lambda g, hh=hh, fc=fc, jj=jj, buf_i=buf_i: g.dma_start(
                    out=mem_bl[buf_i][:, fc * 512 + jj * 128: fc * 512 + (jj + 1) * 128],
                    in_=memin[hh, fc * 128:(fc + 1) * 128, :]),
                    waits=(wait0 if (fc == 0 and jj == 0) else ()), inc="smb", incn=16)
        return sch.val("smb")

    st0.close()

    xbf_users = [0, 0]
    wmb_users = [0, 0]
    mem_users = [0, 0]
    dstg_users = [0, 0]
    delta_dma_prev = 0
    dn_dma_prev = 0
    xext_free = 0
    s_relu_prev = 0
    v_dn = v_dn0
    dn_ev = 0

    for T in range(NT):
        t0 = T * NCHT
        s_relu_prev = max(s_relu_prev, sch.val("ss"))
        for half in range(2):
            for ci in range(2):
                cc = 2 * half + ci
                n = t0 + cc
                bi = n % 2
                sch.op("gpsimd", lambda g, n=n, bi=bi: g.dma_start(
                    out=xbf[bi][:, :], in_=xin[n * CH:(n + 1) * CH, :]),
                    waits=[("sv", xbf_users[bi]), ("sv", v_bcasts)], inc="sxc", incn=16)
                gx = sch.val("sxc")
                sch.op("vector", lambda v, bi=bi: v.reduce_sum(tiny[:, 0:1], xbf[bi][:, :], axis=AX.X),
                       waits=[("sxc", gx)])
                sch.op("vector", lambda v, bi=bi: v.scalar_tensor_tensor(
                    h_bf[:, :], xbf[bi][:, :], 1.0, xbf[bi][:, :], ALU.mult, ALU.mult,
                    accum_out=tiny[:, 1:2]))
                sch.op("vector", lambda v: v.tensor_scalar(tiny[:, 2:3], tiny[:, 0:1], ZC, None, ALU.mult), tight=True)
                sch.op("vector", lambda v: v.tensor_scalar(tiny[:, 3:4], tiny[:, 1:2], ZC, None, ALU.mult), tight=True)
                sch.op("vector", lambda v: v.tensor_tensor(tiny[:, 4:5], tiny[:, 2:3], tiny[:, 2:3], ALU.mult), tight=True)
                sch.op("vector", lambda v: v.tensor_tensor(tiny[:, 5:6], tiny[:, 3:4], tiny[:, 4:5], ALU.subtract), tight=True)
                sch.op("vector", lambda v: v.tensor_scalar(tiny[:, 9:10], tiny[:, 5:6], LNEPS, None, ALU.add), tight=True)
                babylonian(tiny[:, 9:10], tiny2[:, 0:1], tiny2[:, 1:2], tiny2[:, 2:3], tiny2[:, 3:4], tiny2[:, 4:5], 8)
                sch.op("vector", lambda v, n=n: v.reciprocal(stats[:, 2 * n:2 * n + 1], tiny2[:, 0:1]), tight=True)
                sch.op("vector", lambda v, n=n: v.scalar_tensor_tensor(
                    stats[:, 2 * n + 1:2 * n + 2], tiny[:, 2:3], -1.0, stats[:, 2 * n:2 * n + 1],
                    ALU.mult, ALU.mult), tight=True, inc="sv")
                v_stats = sch.val("sv")
                sch.op("scalar", lambda s, bi=bi, n=n: s.activation(
                    h_bf[:, :], xbf[bi][:, :], AF.Identity,
                    scale=stats[:, 2 * n:2 * n + 1], bias=stats[:, 2 * n + 1:2 * n + 2]),
                    waits=[("sv", v_stats)], inc="ss")
                s_xn = sch.val("ss")
                sch.op("vector", lambda v: v.tensor_tensor(h_tmp[:, :], h_bf[:, :], A16[:, :], ALU.mult),
                       waits=[("ss", s_xn)])
                sch.op("vector", lambda v: v.tensor_tensor(h_bf[:, :], h_tmp[:, :], sa16[:, :], ALU.add), inc="sv")
                v_h = sch.val("sv")
                xbf_users[bi] = v_h
                for hh in range(2):
                    tm = sch.op("tensor", lambda t, hh=hh: [
                        t.transpose(ps_tr[:, k * 128:(k + 1) * 128],
                                    h_bf[:, (hh * 8 + k) * 128:(hh * 8 + k + 1) * 128], identb[:, :])
                        for k in range(8)][-1],
                        waits=[("sv", v_h), ("sv", sch.val("sv")), ("sgd", g_ident)], inc="st")
                    sch.op("vector", lambda v, hh=hh, ci=ci: v.tensor_copy(
                        hfm[:, ci * D + hh * 1024: ci * D + (hh + 1) * 1024], ps_tr[:, :]),
                        waits=[("st", tm)], inc="sv")
                v_hfm = sch.val("sv")

                def emit_kv(t, ci=ci):
                    for fc in range(16):
                        t.matmul(ps_k[:, :], hfm[:, ci * D + fc * 128: ci * D + (fc + 1) * 128],
                                 wkv[:, fc * 1024: fc * 1024 + 512], start=(fc == 0), stop=False)
                    t.matmul(ps_k[:, :], ones1[0:1, :], b1[0:1, 0:512], start=False, stop=True)
                    for fc in range(16):
                        t.matmul(ps_v[:, :], hfm[:, ci * D + fc * 128: ci * D + (fc + 1) * 128],
                                 wkv[:, fc * 1024 + 512: fc * 1024 + 1024], start=(fc == 0), stop=False)
                    return t.matmul(ps_v[:, :], ones1[0:1, :], b1[0:1, 512:1024], start=False, stop=True)
                tmk = sch.op("tensor", emit_kv,
                             waits=[("sv", v_hfm), ("ss", s_relu_prev), ("sv", sch.val("sv")),
                                    ("sgd", g_wkv)], inc="st")
                k3 = ps_k.ap().rearrange("p (h c) -> p h c", h=HKV)
                xe = xext.ap().rearrange("p (h c) -> p h c", h=HKV)
                sch.op("scalar", lambda s, k3=k3, xe=xe: s.activation(xe[:, :, 4:132], k3[:, :, :], AF.Relu),
                       waits=[("st", tmk), ("sv", xext_free)])
                sch.op("scalar", lambda s, k3=k3, xe=xe: s.activation(xe[:, :, 132:260], k3[:, :, :], AF.Relu, scale=-1.0))
                sch.op("scalar", lambda s, k3=k3, xe=xe: s.activation(xe[:, :, 1:4], k3[:, :, 125:128], AF.Relu, scale=-1.0),
                       inc="ss")
                s_xext = sch.val("ss")
                s_relu_prev = s_xext
                sch.op("vector", lambda v, cc=cc: v.tensor_copy(vt[:, cc * 512:(cc + 1) * 512], ps_v[:, :]),
                       waits=[("st", tmk)], inc="sv")
                dp3 = dpk.ap().rearrange("p (c h j f) -> p c h j f", c=NCHT, h=HKV, j=3)
                for ji, off in enumerate((3, 2, 1)):
                    sch.op("vector", lambda v, cc=cc, ji=ji, off=off, dp3=dp3, xe=xe: v.tensor_tensor(
                        dp3[:, cc, :, ji, :], xe[:, :, 4:260], xe[:, :, off:off + 256], ALU.mult),
                        waits=[("ss", s_xext)])
                for hh in range(HKV):
                    sch.op("vector", lambda v, cc=cc, hh=hh: v.scalar_tensor_tensor(
                        trash7[:, :], dpk[:, (cc * HKV + hh) * F:(cc * HKV + hh + 1) * F], 1.0,
                        dpk[:, (cc * HKV + hh) * F:(cc * HKV + hh + 1) * F], ALU.mult, ALU.mult,
                        accum_out=n2k[:, cc * HKV + hh:cc * HKV + hh + 1]))
                sch.op("vector", lambda v: v.tensor_copy(mark[:, 4:5], mark[:, 0:1]), inc="sv")
                xext_free = sch.val("sv")
            for sl in range(8):
                bi2 = sl % 2
                lw = sch.op("sync", lambda s, sl=sl, bi2=bi2: s.dma_start(
                    out=wmb_sl[bi2].ap().rearrange("p (fc o) -> p fc o", fc=16)[:, :, :],
                    in_=wmb_s.ap().rearrange("(fc p) o -> p fc o", p=128)[:, :, sl * 256:(sl + 1) * 256]),
                    waits=[("st", wmb_users[bi2]), ("swc", g_wmb_done)], inc="sx", incn=16)
                for ci in range(2):
                    cc = 2 * half + ci
                    psm = ps_m1 if sl % 2 == 0 else ps_m2
                    def emit_mb(t, ci=ci, bi2=bi2, psm=psm, sl=sl):
                        for fc in range(16):
                            t.matmul(psm[:, 0:256], hfm[:, ci * D + fc * 128: ci * D + (fc + 1) * 128],
                                     wmb_sl[bi2][:, fc * 256:(fc + 1) * 256], start=(fc == 0), stop=False)
                        return t.matmul(psm[:, 0:256], ones1[0:1, :],
                                        b1[0:1, 1024 + sl * 256:1024 + (sl + 1) * 256], start=False, stop=True)
                    tmm = sch.op("tensor", emit_mb, waits=[("sx", lw), ("ss", sch.val("ss"))], inc="st")
                    wmb_users[bi2] = tmm
                    sch.op("scalar", lambda s, cc=cc, sl=sl, psm=psm: s.activation(
                        mbt[:, cc * D + sl * 256: cc * D + (sl + 1) * 256], psm[:, 0:256], AF.Sigmoid),
                        waits=[("st", tmm)], inc="ss")

        babylonian(n2k[:, :], nk[:, :], tiny2[:, 8:24], tiny2[:, 24:40], tiny2[:, 40:56], tiny2[:, 56:72], 26, waits=[cur("sv")])
        sch.op("vector", lambda v: v.reciprocal(rk[:, :], nk[:, :]), tight=True)
        sch.op("vector", lambda v: v.tensor_scalar(epsn[:, :], nk[:, :], EPS, None, ALU.mult), tight=True)
        sch.op("vector", lambda v: v.tensor_tensor(tiny[:, 32:48], nk[:, :], rk[:, :], ALU.mult), tight=True)
        sch.op("vector", lambda v: v.tensor_tensor(tiny[:, 48:64], tiny[:, 32:48], tiny[:, 32:48], ALU.mult), tight=True)
        sch.op("vector", lambda v: v.reciprocal(tiny[:, 64:80], tiny[:, 48:64]), tight=True, inc="sv")

        memv = [0, 0, 0, 0]
        memv[0] = emit_mem_block(0, 0, [("st", mem_users[0])])
        memv[1] = emit_mem_block(1, 1, [("st", mem_users[1])])
        prev_evac = sch.val("ss")
        for gq in range(4):
            mbuf = gq % 2
            if gq >= 2:
                memv[gq] = emit_mem_block(gq, mbuf, [("st", mem_users[mbuf])])
            for cc in range(NCHT):
                tm = sch.op("tensor", lambda t, cc=cc, gq=gq: [
                    t.transpose(ps_tr[:, k * 128:(k + 1) * 128],
                                dpk[:, (cc * HKV + gq) * F + k * 128:(cc * HKV + gq) * F + (k + 1) * 128],
                                identb[:, :]) for k in range(6)][-1],
                    waits=[("sv", sch.val("sv"))], inc="st")
                sch.op("vector", lambda v: v.tensor_copy(dpkfm[:, :], ps_tr[:, 0:768]),
                       waits=[("st", tm)], inc="sv")
                v_fm = sch.val("sv")

                def emit_num(t, gq=gq, mbuf=mbuf):
                    for fc in range(6):
                        t.matmul(ps_num[:, :], dpkfm[:, fc * 128:(fc + 1) * 128],
                                 mem_bl[mbuf][:, fc * 512:(fc + 1) * 512], start=(fc == 0), stop=(fc == 5))
                    last = None
                    for fc in range(6):
                        last = t.matmul(ps_den[:, 4 * gq:4 * gq + 4], dpkfm[:, fc * 128:(fc + 1) * 128],
                                        normb[:, fc * 16 + 4 * gq: fc * 16 + 4 * gq + 4],
                                        start=(fc == 0), stop=(fc == 5))
                    return last
                tmn = sch.op("tensor", emit_num,
                             waits=[("sv", v_fm), ("smb", memv[gq]), ("ss", prev_evac)], inc="st")
                mem_users[mbuf] = tmn
                sch.op("vector", lambda v, cc=cc, gq=gq: v.tensor_scalar(
                    den16[:, 4 * gq:4 * gq + 4], ps_den[:, 4 * gq:4 * gq + 4],
                    epsn[:, cc * 4 + gq:cc * 4 + gq + 1], None, ALU.add),
                    waits=[("st", tmn)], tight=True)
                sch.op("vector", lambda v, gq=gq: v.reciprocal(rden[:, 4 * gq:4 * gq + 4], den16[:, 4 * gq:4 * gq + 4]),
                       tight=True)
                sch.op("vector", lambda v, cc=cc, gq=gq: v.tensor_scalar(
                    tiny[:, 8:12], den16[:, 4 * gq:4 * gq + 4], rk[:, cc * 4 + gq:cc * 4 + gq + 1], None, ALU.mult),
                    tight=True)
                sch.op("vector", lambda v, cc=cc, gq=gq: v.tensor_scalar(
                    tiny[:, 16:20], tiny[:, 8:12], tiny[:, 64 + cc * 4 + gq:64 + cc * 4 + gq + 1].bitcast(F32)
                    if False else tiny[:, 64 + cc * 4 + gq:65 + cc * 4 + gq], None, ALU.mult), tight=True)
                sch.op("vector", lambda v: v.tensor_scalar(tiny[:, 8:12], tiny[:, 16:20], -1.0, 1.0, ALU.mult, ALU.add),
                       tight=True)
                sch.op("vector", lambda v, cc=cc, gq=gq: v.tensor_scalar(
                    c_t[:, cc * 16 + 4 * gq:cc * 16 + 4 * gq + 4], tiny[:, 8:12],
                    0.0, rk[:, cc * 4 + gq:cc * 4 + gq + 1], ALU.max, ALU.mult), tight=True, inc="sv")
                v_cp = sch.val("sv")

                def emit_prev(s, gq=gq):
                    last = None
                    for jj in range(4):
                        last = s.activation(prevb[:, jj * 128:(jj + 1) * 128],
                                            ps_num[:, jj * 128:(jj + 1) * 128], AF.Identity,
                                            scale=rden[:, 4 * gq + jj:4 * gq + jj + 1])
                    return last
                sp = sch.op("scalar", emit_prev, waits=[("sv", v_cp)], inc="ss")
                prev_evac = sp
                for jj in range(4):
                    hfull = 4 * gq + jj
                    sch.op("vector", lambda v, jj=jj, cc=cc, gq=gq: v.tensor_tensor(
                        mvb[:, :], prevb[:, jj * 128:(jj + 1) * 128],
                        vt[:, cc * 512 + gq * 128:cc * 512 + (gq + 1) * 128], ALU.subtract),
                        waits=[("ss", sp)])
                    sch.op("vector", lambda v, cc=cc, gq=gq, hfull=hfull: v.scalar_tensor_tensor(
                        wmvt[:, cc * D + hfull * 128: cc * D + (hfull + 1) * 128],
                        mvb[:, :], rk[:, cc * 4 + gq:cc * 4 + gq + 1],
                        mbt[:, cc * D + hfull * 128: cc * D + (hfull + 1) * 128],
                        ALU.mult, ALU.mult))
                sch.op("vector", lambda v: v.tensor_copy(mark[:, 5:6], mark[:, 1:2]), inc="sv")

            def emit_dnorm(t, gq=gq):
                for cc in range(NCHT):
                    t.matmul(ps_dl[0:4, 0:512], c_t[:, cc * 16 + 4 * gq:cc * 16 + 4 * gq + 4],
                             dpk[:, (cc * HKV + gq) * F:(cc * HKV + gq) * F + 512],
                             start=(cc == 0), stop=(cc == NCHT - 1))
                last = None
                for cc in range(NCHT):
                    last = t.matmul(ps_m2[0:4, 0:256], c_t[:, cc * 16 + 4 * gq:cc * 16 + 4 * gq + 4],
                                    dpk[:, (cc * HKV + gq) * F + 512:(cc * HKV + gq) * F + 768],
                                    start=(cc == 0), stop=(cc == NCHT - 1))
                return last
            tdn = sch.op("tensor", emit_dnorm, waits=[("sv", sch.val("sv")), ("ss", dn_ev)], inc="st")
            def emit_dnev(s, gq=gq):
                s.activation(dstg[0][0:4, 0:512], ps_dl[0:4, 0:512], AF.Identity)
                return s.activation(dstg[0][0:4, 512:768], ps_m2[0:4, 0:256], AF.Identity)
            sdn = sch.op("scalar", emit_dnev, waits=[("st", tdn), ("sda", dn_dma_prev)], inc="ss")
            dn_ev = sdn
            def emit_dndma(g, gq=gq, T=T):
                return g.dma_start(out=dpart.ap()[T, 128:129, gq * 3072:(gq + 1) * 3072],
                                   in_=dstg[0][0:4, 0:768])
            sch.op("gpsimd", emit_dndma, waits=[("ss", sdn)], inc="sda", incn=16)
            dn_dma_prev = sch.val("sda")
        v_dn = sch.val("sv")

        bankpairs = [(ps_dl, ps_v), (ps_k, ps_m1)]
        ev_prev = [0, 0]
        for hh in range(H):
            gq = hh // 4
            pa, pb = bankpairs[hh % 2]

            def emit_delta(t, hh=hh, gq=gq, pa=pa, pb=pb):
                for cc in range(NCHT):
                    t.matmul(pa[:, 0:512], wmvt[:, cc * D + hh * 128: cc * D + (hh + 1) * 128],
                             dpk[:, (cc * HKV + gq) * F:(cc * HKV + gq) * F + 512],
                             start=(cc == 0), stop=(cc == NCHT - 1))
                last = None
                for cc in range(NCHT):
                    last = t.matmul(pb[:, 0:256], wmvt[:, cc * D + hh * 128: cc * D + (hh + 1) * 128],
                                    dpk[:, (cc * HKV + gq) * F + 512:(cc * HKV + gq) * F + 768],
                                    start=(cc == 0), stop=(cc == NCHT - 1))
                return last
            td = sch.op("tensor", emit_delta,
                        waits=[("sv", sch.val("sv")), ("ss", ev_prev[hh % 2]), ("ss", dn_ev)], inc="st")

            def emit_dev(s, hh=hh, pa=pa, pb=pb):
                s.activation(dstg[hh % 2][:, 0:512], pa[:, 0:512], AF.Identity)
                return s.activation(dstg[hh % 2][:, 512:768], pb[:, 0:256], AF.Identity)
            sev = sch.op("scalar", emit_dev,
                         waits=[("st", td), ("sda", dstg_users[hh % 2]), ("sda", dn_dma_prev)], inc="ss")
            ev_prev[hh % 2] = sev
            sch.op("gpsimd", lambda g, hh=hh, T=T: g.dma_start(
                out=dpart.ap()[T, 0:128, hh * F:(hh + 1) * F], in_=dstg[hh % 2][:, :]),
                waits=[("ss", sev)], inc="sda", incn=16)
            dstg_users[hh % 2] = sch.val("sda")
        delta_dma_prev = sch.val("sda")

        wsrc, wdst, tb = (wqT, wq_s, T) if T < 4 else (wgT, wg_s, T - 4)
        for rb in range(4 * tb, 4 * tb + 4):
            sch.op("gpsimd", lambda g, wsrc=wsrc, wdst=wdst, rb=rb: g.dma_start(
                out=wdst.ap()[rb * 128:(rb + 1) * 128, :], in_=wsrc[rb * 128:(rb + 1) * 128, :]),
                inc="swc", incn=16)
    g_wqg_done = sch.val("swc")

    psum_stk = contextlib.ExitStack()
    sacc = [sb(f"sacc{i}", [128, 1536], F32, psum_stk) for i in range(2)]
    sld = [sb(f"sld{i}", [128, 1536], F32, psum_stk) for i in range(2)]
    dnld = sb("dnld", [16, 768], F32, psum_stk)
    dnac = [sb(f"dnac{i}", [16, 768], F32, psum_stk) for i in range(2)]
    sum_dma_base = sch.val("sda")
    for q in range(8):
        lo = q * 1536
        ldv = [0, 0]
        ldv[0] = sch.op("gpsimd", lambda g, q=q, lo=lo: g.dma_start(
            out=sacc[0][:, :], in_=dpart.ap()[0, 0:128, lo:lo + 1536]),
            waits=[cur("sda"), cur("sv")], inc="sda", incn=16)
        accw = 0
        for T in range(1, NT):
            bi3 = T % 2
            ldv[bi3] = sch.op("gpsimd", lambda g, T=T, lo=lo, bi3=bi3: g.dma_start(
                out=sld[bi3][:, :], in_=dpart.ap()[T, 0:128, lo:lo + 1536]),
                waits=[("sv", accw)], inc="sda", incn=16)
            accw = sch.op("vector", lambda v, T=T, bi3=bi3: v.tensor_tensor(
                sacc[T % 2][:, :], sacc[(T - 1) % 2][:, :], sld[bi3][:, :], ALU.add),
                waits=[("sda", ldv[bi3])], inc="sv")
        sch.op("gpsimd", lambda g, lo=lo: g.dma_start(
            out=ar2_in.ap()[0:128, lo:lo + 1536], in_=sacc[(NT - 1) % 2][:, :]),
            waits=[("sv", accw)], inc="sda", incn=16)
    dv = sch.op("gpsimd", lambda g: g.dma_start(out=dnac[0][0:16, :], in_=dpart.ap()[0, 128:129, 0:12288]),
                waits=[cur("sv")], inc="sda", incn=16)
    accw = 0
    for T in range(1, NT):
        dv = sch.op("gpsimd", lambda g, T=T: g.dma_start(out=dnld[0:16, :], in_=dpart.ap()[T, 128:129, 0:12288]),
                    waits=[("sv", accw), ("sda", dv)], inc="sda", incn=16)
        accw = sch.op("vector", lambda v, T=T: v.tensor_tensor(
            dnac[T % 2][0:16, :], dnac[(T - 1) % 2][0:16, :], dnld[0:16, :], ALU.add),
            waits=[("sda", dv)], inc="sv")
    sch.op("gpsimd", lambda g: g.dma_start(out=ar2_in.ap()[128:129, 0:12288], in_=dnac[(NT - 1) % 2][0:16, :]),
           waits=[("sv", accw)], inc="sda", incn=16)
    psum_stk.close()

    sch.op("gpsimd", lambda g: g.collective_compute(
        "AllReduce", ALU.add, replica_groups=[list(range(NCORES))],
        ins=[ar2_in.ap().opt()], outs=[ar2_out.ap().opt()]),
        waits=[cur("sgd"), cur("sda"), cur("sxc")], inc="scc")
    cc_ar2 = sch.val("scc")

    BARW = [("st", sch.val("st")), ("sv", sch.val("sv")), ("ss", sch.val("ss")),
            ("sx", sch.val("sx")), ("sgd", sch.val("sgd"))]
    p1.close()

    p2 = contextlib.ExitStack()
    new_rhs = sb("new_rhs", [128, H * 6 * 129], BF16, p2)
    ga16 = sb("ga16", [128, D], BF16, p2)
    B32 = sb("B32", [128, D], F32, p2)
    sbv32 = sb("sbv32", [128, D], F32, p2)
    b2 = sb("b2", [1, 4096], BF16, p2)
    xf = [sb(f"xf{i}", [128, D], F32, p2) for i in range(2)]
    h2bf = sb("h2bf", [128, D], BF16, p2)
    hfm2 = sb("hfm2", [128, D], BF16, p2)
    xeq = sb("xeq", [128, H * 264], BF16, p2)
    dpq = sb("dpq", [128, H * F], BF16, p2)
    gate = sb("gate", [128, D], BF16, p2)
    mh = sb("mh", [128, D], BF16, p2)
    x1 = sb("x1", [128, D], F32, p2)
    xn2 = sb("xn2", [128, D], F32, p2)
    h2f = sb("h2f", [128, D], F32, p2)
    xq = sb("xq", [128, D], BF16, p2)
    xqfm = sb("xqfm", [128, D], BF16, p2)
    dpqfm = sb("dpqfm", [128, F], BF16, p2)
    g2buf = sb("g2buf", [128, D], BF16, p2)
    wsl2 = [sb(f"wsl2_{i}", [128, 16 * 512], BF16, p2) for i in range(2)]
    n2q = sb("n2q", [128, 16], F32, p2)
    nq = sb("nq", [128, 16], F32, p2)
    epsq = sb("epsq", [128, 16], F32, p2)
    sqtr = sb("sqtr", [128, F], BF16, p2)

    ps2_a = psum("ps2_a", [128, 512], F32, p2)
    ps2_b = psum("ps2_b", [128, 512], F32, p2)
    ps2_tr = psum("ps2_tr", [128, 1024], BF16, p2)
    ps2_n1 = psum("ps2_n1", [128, 512], F32, p2)
    ps2_n2 = psum("ps2_n2", [128, 512], F32, p2)
    ps2_d1 = psum("ps2_d1", [128, 512], F32, p2)
    ps2_d2 = psum("ps2_d2", [128, 512], F32, p2)

    sch.barrier(BARW)

    gap = contextlib.ExitStack()
    seg2 = x1
    vrow2 = xn2
    srow = h2f
    vrow2b = xq
    gbH32 = xf[0]
    wfsl = xf[1]
    vrsl = x1
    g1s = h2f
    wrout = g2buf
    dblk = xn2.ap()[:, 0:768]
    mblk = xn2.ap()[:, 768:896]
    dn4 = dpq.ap().bitcast(F32)[:, 0:768]
    normf = tiny2.ap()[:, 0:96]


    def load_seg2(seg):
        sch.op("gpsimd", lambda g, seg=seg: g.dma_start(
            out=seg2[0:1, :], in_=rs1_out.ap()[0:1, seg * D:(seg + 1) * D]),
            waits=[cur("sv")], inc="sgd", incn=16)
        return sch.val("sgd")

    def bcast_p2(row_ap, dst, fp32):
        lhs = onesf if fp32 else ones1
        for gq in range(4):
            tm = sch.op("tensor", lambda t, gq=gq, row_ap=row_ap, lhs=lhs: t.matmul(
                ps2_a[:, :], lhs[0:1, :], row_ap[0:1, gq * 512:(gq + 1) * 512], start=True, stop=True),
                waits=[("sv", sch.val("sv"))], inc="st")
            sch.op("vector", lambda v, gq=gq, dst=dst: v.tensor_copy(dst[:, gq * 512:(gq + 1) * 512], ps2_a[:, :]),
                   waits=[("st", tm)], inc="sv")
        return sch.val("sv")

    rowX = xf[1][0:1, :]

    def clamp_seg(seg, half=False):
        g_sg = load_seg2(seg)
        sch.op("vector", lambda v: v.tensor_scalar(
            srow[0:1, :], seg2[0:1, :], -1.0, None, ALU.mult), waits=[("sgd", g_sg)])
        sch.op("vector", lambda v: v.tensor_tensor(
            rowX, seg2[0:1, :], srow[0:1, :], ALU.max))
        if half:
            sch.op("vector", lambda v: v.tensor_scalar(srow[0:1, :], rowX, 0.25, 0.5, ALU.max, ALU.mult))
        else:
            sch.op("vector", lambda v: v.tensor_scalar(srow[0:1, :], rowX, 0.25, None, ALU.max))
        sch.op("vector", lambda v: v.tensor_scalar(
            rowX, seg2[0:1, :], 0.0, 2.0, ALU.is_ge, ALU.mult))
        sch.op("vector", lambda v: v.tensor_scalar(vrow2[0:1, :], rowX, -1.0, None, ALU.add))
        sch.op("vector", lambda v: v.tensor_tensor(rowX, srow[0:1, :], vrow2[0:1, :], ALU.mult))
        sch.op("vector", lambda v: v.tensor_scalar(vrow2[0:1, :], rowX, 0.0, None, ALU.add))

    clamp_seg(2)
    sch.op("vector", lambda v: v.tensor_copy(vrow2b[0:1, :], vrow2[0:1, :]), inc="sv")
    bcast_p2(vrow2b, ga16, fp32=False)
    clamp_seg(5, half=True)
    sch.op("vector", lambda v: v.tensor_copy(srow[0:1, 0:1], vrow2[0:1, 0:1]), inc="sv")
    bcast_p2(vrow2, gbH32, fp32=True)
    g_sg = load_seg2(4)
    sch.op("vector", lambda v: v.tensor_scalar(vrow2[0:1, :], seg2[0:1, :], 1.0, None, ALU.add),
           waits=[("sgd", g_sg)], inc="sv")
    bcast_p2(vrow2, B32, fp32=True)
    g_sg = load_seg2(3)
    sch.op("vector", lambda v: v.tensor_scalar(vrow2[0:1, :], seg2[0:1, :], 0.0, None, ALU.add),
           waits=[("sgd", g_sg)], inc="sv")
    v_gapb = bcast_p2(vrow2, sbv32, fp32=True)
    sch.op("gpsimd", lambda g: g.dma_start(out=b2[0:1, 0:4096], in_=biases[0:1, 3072:7168]), inc="sgd", incn=16)
    g_b2 = sch.val("sgd")

    wr_v = v_gapb
    g_wf = 0
    g2s = xn2
    for sl in range(16):
        sch.op("sync", lambda s, sl=sl: s.dma_start(out=wfsl[:, :], in_=wfT[sl * 128:(sl + 1) * 128, :]),
               waits=[("sv", wr_v)], inc="sx", incn=16)
        lx = sch.op("sync", lambda s, sl=sl: s.dma_start(out=vrsl[:, :], in_=varT[sl * 128:(sl + 1) * 128, :]),
                    inc="sx", incn=16)
        sch.op("vector", lambda v: v.tensor_scalar(g1s[:, :], wfsl[:, :], 0.2, None, ALU.is_gt),
               waits=[("sx", lx)])
        sch.op("vector", lambda v: v.tensor_scalar(g2s[:, :], wfsl[:, :], -0.2, None, ALU.is_lt))
        sch.op("vector", lambda v: v.tensor_tensor(wfsl[:, :], g1s[:, :], g2s[:, :], ALU.subtract))
        sch.op("vector", lambda v: v.scalar_tensor_tensor(g1s[:, :], wfsl[:, :], 0.6, vrsl[:, :],
                                                          ALU.mult, ALU.add))
        sch.op("vector", lambda v: v.tensor_tensor(wrout[:, :], g1s[:, :], gbH32[:, :], ALU.mult), inc="sv")
        wr_v = sch.val("sv")
        g_wf = sch.op("gpsimd", lambda g, sl=sl: g.dma_start(
            out=wf_s.ap()[sl * 128:(sl + 1) * 128, :], in_=wrout[:, :]),
            waits=[("sv", wr_v)], inc="swc", incn=16)

    sch.op("vector", lambda v: v.memset(dn4[:, :], 0.0), waits=[("swc", g_wf)], inc="sv")
    sch.op("gpsimd", lambda g: g.dma_start(out=dn4[0:16, :], in_=ar2_out.ap()[128:129, 0:12288]),
           waits=[("scc", cc_ar2), (cur("sv"))], inc="sgd", incn=16)
    g_dn4 = sch.val("sgd")
    for fc in range(6):
        sch.op("sync", lambda s, fc=fc: s.dma_start(out=normf[:, fc * 16:(fc + 1) * 16],
                                                    in_=normT[fc * 128:(fc + 1) * 128, :]), inc="sx", incn=16)
    x_nf = sch.val("sx")
    ev_v = sch.val("sv")
    for hh in range(H):
        lx = sch.op("sync", lambda s, hh=hh: s.dma_start(
            out=dblk[:, :], in_=ar2_out.ap()[0:128, hh * F:(hh + 1) * F]),
            waits=[("sv", ev_v), ("scc", cc_ar2)], inc="sx", incn=16)
        for fc in range(6):
            lm = sch.op("sync", lambda s, hh=hh, fc=fc: s.dma_start(
                out=mblk[:, :], in_=memin[hh, fc * 128:(fc + 1) * 128, :]), inc="sx", incn=16)
            tm = sch.op("tensor", lambda t, fc=fc: t.transpose(
                ps2_a[:, 0:128], dblk[:, fc * 128:(fc + 1) * 128], identf[:, :]),
                waits=[("sx", lx), ("sv", ev_v)], inc="st")
            sch.op("vector", lambda v, hh=hh, fc=fc: v.tensor_tensor(
                new_rhs[:, (hh * 6 + fc) * 129:(hh * 6 + fc) * 129 + 128],
                mblk[:, :], ps2_a[:, 0:128], ALU.subtract),
                waits=[("st", tm), ("sx", lm)], inc="sv")
            ev_v = sch.val("sv")
    for fc in range(6):
        tm = sch.op("tensor", lambda t, fc=fc: t.transpose(
            ps2_a[:, 0:128], dn4[:, fc * 128:(fc + 1) * 128], identf[:, :]),
            waits=[("sgd", g_dn4), ("sv", ev_v)], inc="st")

        def emit_nn(v, fc=fc):
            dst = bass.AP(new_rhs.ap().tensor, fc * 129 + 128,
                          [[new_rhs.ap().ap[0][0], 128], [6 * 129, 16]])
            return v.tensor_tensor(dst, normf[:, fc * 16:(fc + 1) * 16],
                                   ps2_a[:, 0:16], ALU.add)
        sch.op("vector", emit_nn, waits=[("st", tm), ("sx", x_nf)], inc="sv")
        ev_v = sch.val("sv")
    v_rhs = sch.val("sv")
    gap.close()

    xf_users = [0, 0]
    wsl2_users = [0, 0]
    for T in range(NT):
        for cc in range(NCHT):
            n = T * NCHT + cc
            bi = n % 2
            sch.op("sync", lambda s, n=n, bi=bi: s.dma_start(
                out=xf[bi][:, :], in_=xin[n * CH:(n + 1) * CH, :]),
                waits=[("sv", xf_users[bi]), ("sv", v_rhs)], inc="sx", incn=16)
            gx = sch.val("sx")
            sch.op("scalar", lambda s, bi=bi, n=n: s.activation(
                h2bf[:, :], xf[bi][:, :], AF.Identity,
                scale=stats[:, 2 * n:2 * n + 1], bias=stats[:, 2 * n + 1:2 * n + 2]),
                waits=[("sx", gx), ("sv", sch.val("sv"))], inc="ss")
            s_xn = sch.val("ss")
            sch.op("vector", lambda v: v.tensor_tensor(gate[:, :], h2bf[:, :], A16[:, :], ALU.mult),
                   waits=[("ss", s_xn)])
            sch.op("vector", lambda v: v.tensor_tensor(h2bf[:, :], gate[:, :], sa16[:, :], ALU.add), inc="sv")
            v_h = sch.val("sv")
            for hhh in range(2):
                tm = sch.op("tensor", lambda t, hhh=hhh: [
                    t.transpose(ps2_tr[:, k * 128:(k + 1) * 128],
                                h2bf[:, (hhh * 8 + k) * 128:(hhh * 8 + k + 1) * 128], identb[:, :])
                    for k in range(8)][-1], waits=[("sv", v_h), ("sv", sch.val("sv"))], inc="st")
                sch.op("vector", lambda v, hhh=hhh: v.tensor_copy(
                    hfm2[:, hhh * 1024:(hhh + 1) * 1024], ps2_tr[:, :]), waits=[("st", tm)], inc="sv")
            v_hfm = sch.val("sv")
            xe = xeq.ap().rearrange("p (h c) -> p h c", h=H)
            ev_done = sch.val("ss")
            for which, wsrc, bias_off in ((0, wq_s, 0), (1, wg_s, 2048)):
                for sl in range(4):
                    sbuf_i = sl % 2
                    lw = sch.op("sync", lambda s, wsrc=wsrc, sl=sl, sbuf_i=sbuf_i: s.dma_start(
                        out=wsl2[sbuf_i].ap().rearrange("p (fc o) -> p fc o", fc=16)[:, :, :],
                        in_=wsrc.ap().rearrange("(fc p) o -> p fc o", p=128)[:, :, sl * 512:(sl + 1) * 512]),
                        waits=[("st", wsl2_users[sbuf_i]), ("swc", g_wqg_done)], inc="sx", incn=16)
                    psp = ps2_a if sl % 2 == 0 else ps2_b

                    def emit_proj(t, sbuf_i=sbuf_i, psp=psp, sl=sl, bias_off=bias_off):
                        for fc in range(16):
                            t.matmul(psp[:, :], hfm2[:, fc * 128:(fc + 1) * 128],
                                     wsl2[sbuf_i][:, fc * 512:(fc + 1) * 512], start=(fc == 0), stop=False)
                        return t.matmul(psp[:, :], ones1[0:1, :],
                                        b2[0:1, bias_off + sl * 512:bias_off + (sl + 1) * 512],
                                        start=False, stop=True)
                    tmp_ = sch.op("tensor", emit_proj,
                                  waits=[("sx", lw), ("sv", v_hfm), ("ss", ev_done), ("sgd", g_b2)], inc="st")
                    wsl2_users[sbuf_i] = tmp_
                    if which == 0:
                        def emit_relu(s, sl=sl, psp=psp, xe=xe):
                            kq = psp.ap().rearrange("p (h c) -> p h c", h=4)
                            s.activation(xe[:, 4 * sl:4 * sl + 4, 4:132], kq[:, :, :], AF.Relu)
                            s.activation(xe[:, 4 * sl:4 * sl + 4, 132:260], kq[:, :, :], AF.Relu, scale=-1.0)
                            return s.activation(xe[:, 4 * sl:4 * sl + 4, 1:4], kq[:, :, 125:128], AF.Relu, scale=-1.0)
                        ev_done = sch.op("scalar", emit_relu, waits=[("st", tmp_)], inc="ss")
                    else:
                        def emit_sig(s, sl=sl, psp=psp):
                            return s.activation(gate[:, sl * 512:(sl + 1) * 512], psp[:, :], AF.Sigmoid)
                        ev_done = sch.op("scalar", emit_sig, waits=[("st", tmp_)], inc="ss")
            s_evq = sch.val("ss")
            sch.op("vector", lambda v: v.tensor_tensor(xq[:, :], gate[:, :], ga16[:, :], ALU.mult),
                   waits=[("ss", s_evq)], inc="sv")
            dq3 = dpq.ap().rearrange("p (h j f) -> p h j f", h=H, j=3)
            for ji, off in enumerate((3, 2, 1)):
                sch.op("vector", lambda v, ji=ji, off=off, dq3=dq3, xe=xe: v.tensor_tensor(
                    dq3[:, :, ji, :], xe[:, :, 4:260], xe[:, :, off:off + 256], ALU.mult))
            sch.op("vector", lambda v: v.tensor_copy(mark[:, 6:7], mark[:, 2:3]), inc="sv")
            v_dpq = sch.val("sv")
            for hh in range(8):
                sch.op("vector", lambda v, hh=hh: v.scalar_tensor_tensor(
                    sqtr[:, :], dpq[:, hh * F:(hh + 1) * F], 1.0, dpq[:, hh * F:(hh + 1) * F],
                    ALU.mult, ALU.mult, accum_out=n2q[:, hh:hh + 1]))
            for hh in range(8, 16):
                sch.op("scalar", lambda s, hh=hh: s.activation(
                    g2buf[:, 0:F], dpq[:, hh * F:(hh + 1) * F], AF.Square,
                    accum_out=n2q[:, hh:hh + 1]), waits=[("sv", v_dpq)], inc="ss")
            s_n2 = sch.val("ss")
            babylonian(n2q[:, :], nq[:, :], tiny2[:, 8:24], tiny2[:, 24:40], tiny2[:, 40:56], tiny2[:, 56:72], 14,
                       waits=[("ss", s_n2), cur("sv")])
            sch.op("vector", lambda v: v.tensor_scalar(epsq[:, :], nq[:, :], EPS, None, ALU.mult), tight=True, inc="sv")
            v_nq = sch.val("sv")
            mh_done = sch.val("ss")
            for hg in range(4):
                psn = ps2_n1 if hg % 2 == 0 else ps2_n2
                psd = ps2_d1 if hg % 2 == 0 else ps2_d2
                for jj in range(4):
                    hh = hg * 4 + jj
                    tm = sch.op("tensor", lambda t, hh=hh: [
                        t.transpose(ps2_tr[:, k * 128:(k + 1) * 128],
                                    dpq[:, hh * F + k * 128: hh * F + (k + 1) * 128], identb[:, :])
                        for k in range(6)][-1], waits=[("sv", v_nq), ("sv", sch.val("sv"))], inc="st")
                    sch.op("vector", lambda v: v.tensor_copy(dpqfm[:, :], ps2_tr[:, 0:768]),
                           waits=[("st", tm)], inc="sv")
                    v_fm = sch.val("sv")

                    def emit_na(t, hh=hh, jj=jj, psn=psn, psd=psd):
                        for fc in range(6):
                            t.matmul(psn[:, jj * 128:(jj + 1) * 128], dpqfm[:, fc * 128:(fc + 1) * 128],
                                     new_rhs[:, (hh * 6 + fc) * 129:(hh * 6 + fc) * 129 + 128],
                                     start=(fc == 0), stop=(fc == 5))
                        last = None
                        for fc in range(6):
                            last = t.matmul(psd[:, jj:jj + 1], dpqfm[:, fc * 128:(fc + 1) * 128],
                                            new_rhs[:, (hh * 6 + fc) * 129 + 128:(hh * 6 + fc) * 129 + 129],
                                            start=(fc == 0), stop=(fc == 5))
                        return last
                    sch.op("tensor", emit_na, waits=[("sv", v_fm), ("ss", mh_done)], inc="st")
                t_na = sch.val("st")
                sch.op("vector", lambda v, hg=hg, psd=psd: v.tensor_tensor(
                    tiny[:, 20:24], psd[:, 0:4], epsq[:, 4 * hg:4 * hg + 4], ALU.add),
                    waits=[("st", t_na)], tight=True)
                sch.op("vector", lambda v: v.reciprocal(tiny[:, 24:28], tiny[:, 20:24]), tight=True, inc="sv")
                v_rc = sch.val("sv")

                def emit_mh(s, hg=hg, psn=psn):
                    last = None
                    for jj in range(4):
                        last = s.activation(mh[:, (hg * 4 + jj) * 128:(hg * 4 + jj + 1) * 128],
                                            psn[:, jj * 128:(jj + 1) * 128], AF.Identity,
                                            scale=tiny[:, 24 + jj:25 + jj])
                    return last
                mh_done = sch.op("scalar", emit_mh, waits=[("sv", v_rc)], inc="ss")
            sch.op("vector", lambda v: v.tensor_tensor(g2buf[:, :], ga16[:, :], h2bf[:, :], ALU.mult),
                   waits=[("ss", mh_done)])
            sch.op("vector", lambda v, bi=bi: v.tensor_tensor(x1[:, :], xf[bi][:, :], g2buf[:, :], ALU.add))
            sch.op("vector", lambda v: v.tensor_tensor(g2buf[:, :], xq[:, :], mh[:, :], ALU.mult))
            sch.op("vector", lambda v: v.tensor_tensor(xn2[:, :], x1[:, :], g2buf[:, :], ALU.add), inc="sv")
            xf_users[bi] = sch.val("sv")
            sch.op("vector", lambda v: v.reduce_sum(tiny[:, 0:1], xn2[:, :], axis=AX.X))
            sch.op("vector", lambda v: v.scalar_tensor_tensor(
                dpq[:, 0:D], xn2[:, :], 1.0, xn2[:, :], ALU.mult, ALU.mult, accum_out=tiny[:, 1:2]))
            sch.op("vector", lambda v: v.tensor_scalar(tiny[:, 2:3], tiny[:, 0:1], ZC, None, ALU.mult), tight=True)
            sch.op("vector", lambda v: v.tensor_scalar(tiny[:, 3:4], tiny[:, 1:2], ZC, None, ALU.mult), tight=True)
            sch.op("vector", lambda v: v.tensor_tensor(tiny[:, 4:5], tiny[:, 2:3], tiny[:, 2:3], ALU.mult), tight=True)
            sch.op("vector", lambda v: v.tensor_tensor(tiny[:, 5:6], tiny[:, 3:4], tiny[:, 4:5], ALU.subtract), tight=True)
            sch.op("vector", lambda v: v.tensor_scalar(tiny[:, 9:10], tiny[:, 5:6], LNEPS, None, ALU.add), tight=True)
            babylonian(tiny[:, 9:10], tiny2[:, 0:1], tiny2[:, 1:2], tiny2[:, 2:3], tiny2[:, 3:4], tiny2[:, 4:5], 16)
            sch.op("vector", lambda v: v.reciprocal(tiny[:, 6:7], tiny2[:, 0:1]), tight=True)
            sch.op("vector", lambda v: v.scalar_tensor_tensor(
                tiny[:, 7:8], tiny[:, 2:3], -1.0, tiny[:, 6:7], ALU.mult, ALU.mult), tight=True, inc="sv")
            v_ln2 = sch.val("sv")
            sch.op("scalar", lambda s: s.activation(x1[:, :], xn2[:, :], AF.Identity,
                                                    scale=tiny[:, 6:7], bias=tiny[:, 7:8]),
                   waits=[("sv", v_ln2)], inc="ss")
            s_xn2 = sch.val("ss")
            sch.op("vector", lambda v: v.tensor_tensor(h2f[:, :], x1[:, :], B32[:, :], ALU.mult),
                   waits=[("ss", s_xn2)])
            sch.op("vector", lambda v: v.tensor_tensor(x1[:, :], h2f[:, :], sbv32[:, :], ALU.add))
            sch.op("vector", lambda v: v.tensor_scalar(xq[:, :], x1[:, :], 0.33, None, ALU.is_gt))
            sch.op("vector", lambda v: v.tensor_scalar(g2buf[:, :], x1[:, :], -0.33, None, ALU.is_lt))
            sch.op("vector", lambda v: v.tensor_tensor(mh[:, :], xq[:, :], g2buf[:, :], ALU.subtract), inc="sv")
            v_xq = sch.val("sv")
            for hhh in range(2):
                tm = sch.op("tensor", lambda t, hhh=hhh: [
                    t.transpose(ps2_tr[:, k * 128:(k + 1) * 128],
                                mh[:, (hhh * 8 + k) * 128:(hhh * 8 + k + 1) * 128], identb[:, :])
                    for k in range(8)][-1], waits=[("sv", v_xq), ("sv", sch.val("sv"))], inc="st")
                sch.op("vector", lambda v, hhh=hhh: v.tensor_copy(
                    xqfm[:, hhh * 1024:(hhh + 1) * 1024], ps2_tr[:, :]), waits=[("st", tm)], inc="sv")
            v_xqfm = sch.val("sv")
            out_add = sch.val("sv")
            for sl in range(4):
                sbuf_i = sl % 2
                lw = sch.op("sync", lambda s, sl=sl, sbuf_i=sbuf_i: s.dma_start(
                    out=wsl2[sbuf_i].ap().rearrange("p (fc o) -> p fc o", fc=16)[:, :, :],
                    in_=wf_s.ap().rearrange("(fc p) o -> p fc o", p=128)[:, :, sl * 512:(sl + 1) * 512]),
                    waits=[("st", wsl2_users[sbuf_i]), ("swc", g_wf)], inc="sx", incn=16)
                psp = ps2_a if sl % 2 == 0 else ps2_b

                def emit_ffn(t, sbuf_i=sbuf_i, psp=psp):
                    last = None
                    for fc in range(16):
                        last = t.matmul(psp[:, :], xqfm[:, fc * 128:(fc + 1) * 128],
                                        wsl2[sbuf_i][:, fc * 512:(fc + 1) * 512],
                                        start=(fc == 0), stop=(fc == 15))
                    return last
                tmf = sch.op("tensor", emit_ffn, waits=[("sx", lw), ("sv", v_xqfm), ("sv", out_add)], inc="st")
                wsl2_users[sbuf_i] = tmf
                sch.op("vector", lambda v, sl=sl, psp=psp: v.tensor_tensor(
                    h2f[:, sl * 512:(sl + 1) * 512], xn2[:, sl * 512:(sl + 1) * 512], psp[:, :], ALU.add),
                    waits=[("st", tmf)], inc="sv")
                out_add = sch.val("sv")
            sch.op("sync", lambda s, n=n: s.dma_start(out=out[n * CH:(n + 1) * CH, :], in_=h2f[:, :]),
                   waits=[("sv", out_add)], inc="sout", incn=16)
            so = sch.val("sout")
            sch.op("vector", lambda v: v.tensor_copy(mark[:, 7:8], mark[:, 3:4]),
                   waits=[("sout", so)], inc="sv")
            xf_users[bi] = sch.val("sv")
    p2.close()

    final_sgd = sch.val("sgd")
    final_sout = sch.val("sout")
    final_sda = sch.val("sda")
    final_swc = sch.val("swc")
    sems = {}
    with contextlib.ExitStack() as sem_stack:
        for name in SEMNAMES:
            sems[name] = sem_stack.enter_context(nc.semaphore(name))
        with nc.Block() as block:
            @block.sync
            def _(eng):
                sch.replay("sync", eng, sems)
                eng.wait_ge(sems["sout"], final_sout)
                eng.wait_ge(sems["sgd"], final_sgd)
                eng.wait_ge(sems["sda"], final_sda)
                eng.wait_ge(sems["swc"], final_swc)

            @block.gpsimd
            def _(eng):
                sch.replay("gpsimd", eng, sems)

            @block.tensor
            def _(eng):
                sch.replay("tensor", eng, sems)

            @block.vector
            def _(eng):
                sch.replay("vector", eng, sems)

            @block.scalar
            def _(eng):
                sch.replay("scalar", eng, sems)
    es.close()
    return nc


_NC_CACHE = None


def kernel(**inputs):
    global _NC_CACHE
    f32 = np.float32
    x = np.ascontiguousarray(np.asarray(inputs["L0_attn_hidden_states"], dtype=f32))
    wkT = np.ascontiguousarray(np.asarray(inputs["L0_attn_Wk"], dtype=f32).T)
    wvT = np.ascontiguousarray(np.asarray(inputs["L0_attn_Wv"], dtype=f32).T)
    wmbT = np.ascontiguousarray(np.asarray(inputs["L0_attn_Wmb"], dtype=f32).T)
    wqT = np.ascontiguousarray(np.asarray(inputs["L0_attn_Wq"], dtype=f32).T)
    wgT = np.ascontiguousarray(np.asarray(inputs["L0_attn_Wg"], dtype=f32).T)
    wfT = np.ascontiguousarray(np.asarray(inputs["L0_ffn_weight"], dtype=f32).T)
    varT = np.ascontiguousarray(np.asarray(inputs["L0_ffn_rram_variation"], dtype=f32).T)
    biases = np.ascontiguousarray(np.concatenate([
        np.asarray(inputs["L0_attn_bk"], dtype=f32),
        np.asarray(inputs["L0_attn_bv"], dtype=f32),
        np.asarray(inputs["L0_attn_bmb"], dtype=f32),
        np.asarray(inputs["L0_attn_bq"], dtype=f32),
        np.asarray(inputs["L0_attn_bg"], dtype=f32)])[None, :])
    memory = np.ascontiguousarray(np.asarray(inputs["L0_attn_memory"], dtype=f32))
    normT = np.ascontiguousarray(np.asarray(inputs["L0_attn_norm"], dtype=f32).T)
    cvals = np.asarray(inputs["L0_c"], dtype=f32)
    wmod = np.asarray(inputs["L0_wmod"], dtype=f32)
    ident = np.eye(128, dtype=f32)

    if _NC_CACHE is None:
        _NC_CACHE = build_nc()
    nc = _NC_CACHE

    in_maps = []
    for b in range(NCORES):
        cl = np.empty((128, 16), dtype=f32)
        for j in range(2):
            cl[:, j * 8:(j + 1) * 8] = cvals[:, (2 * b + j) * 128:(2 * b + j + 1) * 128].T
        in_maps.append(dict(
            x=np.ascontiguousarray(x[b]),
            wkT=wkT, wvT=wvT, wmbT=wmbT, wqT=wqT, wgT=wgT, wfT=wfT, varT=varT,
            biases=biases, memory=memory, normT=normT,
            clocal=np.ascontiguousarray(cl),
            wmodK=np.ascontiguousarray(wmod[256 * b:256 * (b + 1), :]),
            ident=ident,
        ))
    res = run_bass_kernel_spmd(nc, in_maps, core_ids=list(range(NCORES)))
    return np.stack([res.results[b]["out"] for b in range(NCORES)], axis=0)
